# version 08
# speedup vs baseline: 1.2282x; 1.0168x over previous
import numpy as np
from contextlib import ExitStack

import concourse.bass as bass
import concourse.mybir as mybir
from concourse.bass_utils import run_bass_kernel_spmd

F32 = mybir.dt.float32
BF16 = mybir.dt.bfloat16
AF = mybir.ActivationFunctionType
ALU = mybir.AluOpType
AX = mybir.AxisListType

S = 2048
DM = 2048
NT = S // 128
ND = DM // 128
NH = 16
EPS = 1e-5
RSQ = float(1.0 / np.sqrt(np.float32(128.0)))
NEG = -1.0e30

FUSED = True
LAST_CNT = None

ENG2SEM = {"tensor": "TE", "vector": "VE", "scalar": "SE", "gpsimd": "GE",
           "sync": "SY"}
DMASEM = {"sync": "DSY", "gpsimd": "DGE", "scalar": "DSE"}
ALLSEMS = ("TE", "VE", "SE", "GE", "SY", "DSY", "DGE", "DSE",
           "DW", "DB", "DX", "DS", "DR", "DA0", "DA1", "DA2", "DA3",
           "DC0", "DC1", "DC2")


class P:
    ENGINES = ("tensor", "vector", "scalar", "gpsimd", "sync")

    def __init__(self, nc):
        self.nc = nc
        self.ops = {e: [] for e in self.ENGINES}
        self.sems = {}
        self.cnt = {s: 0 for s in ALLSEMS}
        self.lastwait = {e: {} for e in self.ENGINES}

    def make_sems(self, ctx):
        for s in ALLSEMS:
            self.sems[s] = ctx.enter_context(self.nc.semaphore(s))

    def op(self, eng, build, waits=(), dma=False, sem=None):
        if sem is None:
            sem = DMASEM[eng] if dma else ENG2SEM[eng]
        amt = 16 if dma else 1
        self.cnt[sem] += amt
        val = self.cnt[sem]
        lw = self.lastwait[eng]
        needed = []
        own = ENG2SEM[eng]
        for t in waits:
            if not t:
                continue
            s, v = t
            if lw.get(s, 0) < v:
                needed.append((s, v))
                lw[s] = v
        sems = self.sems

        def f(e, needed=needed, build=build, sem=sem, amt=amt):
            for s, v in needed:
                e.wait_ge(sems[s], v)
            ins = build(e)
            ins.then_inc(sems[sem], amt)
        self.ops[eng].append(f)
        return (sem, val)

    def raw(self, eng, build):
        self.ops[eng].append(build)

    def barrier(self):
        snap = {s: v for s, v in self.cnt.items() if v}
        for eng in self.ENGINES:
            lw = self.lastwait[eng]
            need = [(s, v) for s, v in snap.items() if lw.get(s, 0) < v]
            for s, v in need:
                lw[s] = v
            sems = self.sems

            def f(e, need=need):
                for s, v in need:
                    e.wait_ge(sems[s], v)
            self.ops[eng].append(f)

    def dma(self, eng, out, in_, waits=(), sem=None, **kw):
        return self.op(eng, lambda e: e.dma_start(out=out, in_=in_, **kw),
                       waits, dma=True, sem=sem)

    def mm(self, out, lhsT, rhs, start, stop, waits=()):
        return self.op("tensor",
                       lambda e: e.matmul(out, lhsT, rhs, start=start,
                                          stop=stop), waits)

    def trans(self, out, in_, ident, waits=()):
        return self.op("tensor",
                       lambda e: e.transpose(out, in_, ident), waits)

    def act(self, out, in_, func, waits=(), bias=0.0, scale=1.0, accum=None):
        if accum is not None:
            return self.op("scalar",
                           lambda e: e.activation(out, in_, func, bias=bias,
                                                  scale=scale,
                                                  accum_out=accum), waits)
        return self.op("scalar",
                       lambda e: e.activation(out, in_, func, bias=bias,
                                              scale=scale), waits)

    def copy(self, eng, out, in_, waits=()):
        if eng == "scalar":
            return self.op(eng, lambda e: e.copy(out, in_), waits)
        return self.op(eng, lambda e: e.tensor_copy(out, in_), waits)

    def tt(self, out, a, b, alu, waits=(), eng="vector"):
        return self.op(eng, lambda e: e.tensor_tensor(out, a, b, alu), waits)

    def ts(self, out, a, s1, s2, op0, op1=None, waits=(), eng="vector"):
        if s2 is None and op1 is None:
            return self.op(eng,
                           lambda e: e.tensor_single_scalar(out, a, s1, op0),
                           waits)
        return self.op(eng,
                       lambda e: e.tensor_scalar(out, a, s1, s2, op0, op1),
                       waits)

    def stt(self, out, in0, scal, in1, op0, op1, waits=(), eng="vector"):
        return self.op(eng,
                       lambda e: e.scalar_tensor_tensor(out, in0, scal, in1,
                                                        op0, op1), waits)

    def red(self, out, in_, alu, waits=(), negate=False, eng="vector"):
        return self.op(eng,
                       lambda e: e.tensor_reduce(out, in_, AX.X, alu,
                                                 negate=negate), waits)

    def recip(self, out, in_, waits=()):
        return self.op("vector", lambda e: e.reciprocal(out, in_), waits)

    def memset(self, eng, ap, val, waits=()):
        return self.op(eng, lambda e: e.memset(ap, val), waits)

    def finalize(self):
        with self.nc.Block() as block:
            for eng in self.ENGINES:
                ops = self.ops[eng]
                getattr(block, eng)(lambda e, ops=ops: [f(e) for f in ops])


def V(t, off, dims):
    return bass.AP(t, off, dims)


class Bufs:

    def __init__(self, nc, ctx):
        E = ctx.enter_context
        self.slab1 = E(nc.sbuf_tensor("slab1", [128, 32768], BF16))
        self.slab2 = E(nc.sbuf_tensor("slab2", [128, 32768], BF16))
        self.slab3 = E(nc.sbuf_tensor("slab3", [128, 4096], F32))
        self.slab4 = E(nc.sbuf_tensor("slab4", [128, 14336], BF16))
        self.slab6 = E(nc.sbuf_tensor("slab6", [128, 6144], BF16))
        self.cols = E(nc.sbuf_tensor("cols", [128, 384], F32))
        self.eyebf = E(nc.sbuf_tensor("eyebf", [128, 128], BF16))
        self.eyef = E(nc.sbuf_tensor("eyef", [128, 128], F32))
        self.masks = E(nc.sbuf_tensor("masksb", [128, 256], BF16))
        self.rowsf = E(nc.sbuf_tensor("rowsf", [16, 128], F32))
        self.rowbf = E(nc.sbuf_tensor("rowbf", [1, 4096], BF16))
        self.onesbf = E(nc.sbuf_tensor("onesbf", [1, 512], BF16))
        self.smallbf = E(nc.sbuf_tensor("smallbf", [16, 512], BF16))
        self.memf = E(nc.sbuf_tensor("memf", [128, 2080], F32))
        self.psA = E(nc.psum_tensor("psA", [128, 2048], F32))
        self.psB = E(nc.psum_tensor("psB", [128, 1024], F32))
        self.psT = E(nc.psum_tensor("psT", [128, 2048], BF16))

    def col(self, j):
        return self.cols[:, j:j + 1]

    @property
    def rowflat(self):
        return V(self.slab3, 2048, [[4096, 1], [1, 2048]])


def load_common(p, b, io):
    t1 = p.dma("sync", b.eyef[:, :], io["eye"][:, :])
    t2 = p.dma("gpsimd", b.eyebf[:, :], io["eye"][:, :])
    t3 = p.dma("gpsimd",
               V(b.masks, 0, [[256, 128], [128, 2], [1, 128]]),
               V(io["masks"], 0, [[128, 128], [16384, 2], [1, 128]]))
    t4 = p.memset("vector", b.onesbf[:, :], 1.0)
    t5 = p.memset("vector", b.cols[:, 62:63], EPS)
    return [t1, t2, t3, t4, t5]


def emit_ln(p, b, io_x, xsT, i, double_ln=False, gb=None, prev=None):
    prev = prev or {}
    par = i % 2
    xt = V(b.slab3, 0, [[4096, 128], [1, 2048]])
    sc32 = V(b.slab3, 2048, [[4096, 128], [1, 2048]])
    xs_tok = V(b.slab4, par * 2048, [[14336, 128], [1, 2048]])
    cb = par * 11
    c_sum, c_sq, c_mu, c_t0, c_v1, c_r1, c_t1, c_a, c_c = [b.col(cb + j)
                                                           for j in range(9)]
    tx = p.dma("sync", xt, io_x[i * 128:(i + 1) * 128, :], sem="DX",
               waits=prev.get("gate", []))
    t_sum = p.red(c_sum, xt, ALU.add, waits=[tx])
    t_sq = p.act(sc32, xt, AF.Square, waits=[tx], accum=c_sq)
    t_mu = p.ts(c_mu, c_sum, 1.0 / DM, None, ALU.mult, waits=[t_sum])
    t_m2 = p.tt(c_t0, c_mu, c_mu, ALU.mult, waits=[t_mu])
    t_v0 = p.ts(c_v1, c_sq, 1.0 / DM, None, ALU.mult, waits=[t_sq])
    t_v1 = p.tt(c_v1, c_v1, c_t0, ALU.subtract, waits=[t_v0, t_m2])
    t_l1 = p.act(c_t1, c_v1, AF.Ln, waits=[t_v1], bias=b.col(62))
    t_r1 = p.act(c_r1, c_t1, AF.Exp, waits=[t_l1], scale=-0.5)
    if double_ln:
        c_v2, c_r2 = b.col(cb + 9), b.col(cb + 10)
        t_a0 = p.tt(c_v2, c_v1, c_r1, ALU.mult, waits=[t_r1])
        t_a1 = p.tt(c_v2, c_v2, c_r1, ALU.mult, waits=[t_a0])
        t_l2 = p.act(c_t1, c_v2, AF.Ln, waits=[t_a1], bias=b.col(62))
        t_r2 = p.act(c_r2, c_t1, AF.Exp, waits=[t_l2], scale=-0.5)
        t_aa = p.tt(c_a, c_r1, c_r2, ALU.mult, waits=[t_r2])
    else:
        t_aa = p.copy("vector", c_a, c_r1, waits=[t_r1])
    t_c0 = p.tt(c_c, c_mu, c_a, ALU.mult, waits=[t_aa, t_mu])
    t_c1 = p.ts(c_c, c_c, -1.0, None, ALU.mult, waits=[t_c0])
    pxs = []
    if double_ln:
        t_t1 = p.act(sc32, xt, AF.Identity, waits=[t_c1, t_aa],
                     bias=c_c, scale=c_a)
        g_ap, b_ap = gb
        t_g = p.tt(xs_tok, sc32, g_ap, ALU.mult, waits=[t_t1] + pxs)
        t_xs = p.tt(xs_tok, xs_tok, b_ap, ALU.add, waits=[t_g])
    else:
        t_xs = p.act(xs_tok, xt, AF.Identity, waits=[t_c1, t_aa] + pxs,
                     bias=c_c, scale=c_a)
    xsT_t, xsT_off = xsT
    ev = prev.get("ev")
    ttok = None
    for g2 in range(2):
        ttoks = []
        for dc in range(8 * g2, 8 * g2 + 8):
            ttoks.append(
                p.trans(V(b.psT, (dc % 8) * 128, [[2048, 128], [1, 128]]),
                        V(xs_tok.tensor, par * 2048 + dc * 128,
                          [[14336, 128], [1, 128]]),
                        b.eyebf[:, :],
                        waits=[t_xs] + ([ev] if ev else [])))
        ttok = ttoks[-1]
        ev = p.copy("vector",
                    V(xsT_t, xsT_off + g2 * 8 * 2048 + i * 128,
                      [[32768, 128], [2048, 8], [1, 128]]),
                    V(b.psT, 0, [[2048, 128], [128, 8], [1, 128]]),
                    waits=[ttok])
    return {"ev": ev, "xt_readers": [t_sum, t_sq, t_xs],
            "par_toks": [t_xs, ttok]}, ev


def ph_l0attn(p, b, io):
    nc = p.nc
    common = load_common(p, b, io)
    g_ap = V(b.slab4, 8192, [[14336, 128], [1, 2048]])
    b_ap = V(b.slab4, 10240, [[14336, 128], [1, 2048]])
    t_gb = p.dma("gpsimd", V(b.slab4, 8192, [[14336, 128], [2048, 2], [1, 2048]]),
                 V(io["l0_gb"], 0, [[2048, 128], [262144, 2], [1, 2048]]))
    p.barrier()

    xsT = (b.slab1, 0)
    hist = []
    for i in range(NT):
        pr = {}
        if hist:
            pr["ev"] = hist[-1]["ev"]
            pr["gate"] = list(hist[-1]["xt_readers"])
        if len(hist) >= 2:
            pr["gate"] = pr.get("gate", []) + hist[-2]["par_toks"]
        d, _ = emit_ln(p, b, io["x_in"], xsT, i, double_ln=True,
                       gb=(g_ap, b_ap), prev=pr)
        hist.append(d)
    lnev = [h_["ev"] for h_ in hist]
    lnlast = [hist[14]["xt_readers"][-1], hist[15]["xt_readers"][-1]]
    lntt = [hist[14]["par_toks"][1], hist[15]["par_toks"][1]]

    xsTv = lambda dc, t0, n: V(b.slab1, dc * 2048 + t0, [[32768, 128], [1, n]])
    qT = V(b.slab4, 0, [[14336, 128], [1, 2048]])
    kT = V(b.slab4, 2048, [[14336, 128], [1, 2048]])
    vh = lambda kt: V(b.slab4, 4096 + kt * 128, [[14336, 128], [1, 128]])
    atT = lambda q0, n: V(b.slab4, 6144 + q0, [[14336, 128], [1, n]])
    whd = lambda dc, c0, n: V(b.slab6, dc * 384 + c0,
                              [[6144, 128], [1, n]])
    brow = lambda c0, n: V(b.smallbf, c0, [[512, 1], [1, n]])
    ones1 = V(b.onesbf, 0, [[512, 1], [1, 512]])

    mask_up = V(b.masks, 0, [[256, 128], [1, 128]])
    mask_low = V(b.masks, 128, [[256, 128], [1, 128]])
    c_nm, c_bias, c_sum, c_lns, c_tmp = [b.col(j) for j in range(16, 21)]

    hw_prev = []
    ev_av = None
    t_lo_prev = None
    t_tr_prev = None
    for h in range(NH):
        t_w = p.dma("gpsimd",
                    V(b.slab6, 0, [[6144, 128], [384, 16], [1, 384]]),
                    V(io["l0_wqkv"], h * 2048 * 384,
                      [[384, 128], [49152, 16], [1, 384]]), sem="DW",
                    waits=hw_prev)
        t_b = p.dma("gpsimd", V(b.smallbf, 0, [[512, 1], [1, 384]]),
                    V(io["l0_bqkv"], h * 384, [[384, 1], [1, 384]]), sem="DB",
                    waits=hw_prev)
        ev_qk = None
        for (dst, c0) in ((qT, 0), (kT, 128)):
            mtoks = []
            for tb in range(4):
                psq = V(b.psA, tb * 512, [[2048, 128], [1, 512]])
                for dc in range(ND):
                    mtoks.append(p.mm(psq, whd(dc, c0, 128),
                                      xsTv(dc, tb * 512, 512),
                                      start=(dc == 0), stop=False,
                                      waits=[t_w, lnev[tb * 4 + 3]]
                                      + ([ev_qk] if ev_qk else
                                         ([ev_av] if ev_av else []))))
                mtoks.append(p.mm(psq,
                                  V(b.smallbf, c0, [[512, 1], [1, 128]]),
                                  ones1, start=False, stop=True,
                                  waits=[t_b]))
            ev_qk = p.copy("vector", dst, b.psA[:, :],
                           waits=[mtoks[-1]] + (lntt if c0 == 0 else []))
        ev_k = ev_qk
        evv = []
        for kt in range(NT):
            psv = V(b.psB, (kt % 2) * 512, [[1024, 128], [1, 128]])
            w = [evv[-2]] if kt >= 2 else []
            mt = []
            for dc in range(ND):
                mt.append(p.mm(psv, xsTv(dc, kt * 128, 128),
                               whd(dc, 256, 128), start=(dc == 0), stop=False,
                               waits=w + [t_w, lnev[kt]]))
            mt.append(p.mm(psv, ones1[0:1, 0:128],
                           V(b.smallbf, 256, [[512, 1], [1, 128]]),
                           start=False, stop=True, waits=[t_b]))
            evv.append(p.copy("vector", vh(kt), psv, waits=[mt[-1]]))

        pair_free = [[], []]
        prev_t = [None, None]
        prev_e = None
        t_rv = None
        for qt in range(NT):
            klen = (qt + 1) * 128
            pp = qt % 2
            cb = pp * 12
            if klen <= 1024:
                chunks = [(0, klen, pp * 1024)]
                pwaits = pair_free[pp]
            else:
                chunks = [(0, 1024, 0), (1024, klen - 1024, 1024)]
                pwaits = pair_free[0] + pair_free[1]
            etoks = []
            stats = []
            for ci, (c0k, ln, boff) in enumerate(chunks):
                m = None
                for kb in range(0, ln, 512):
                    n = min(512, ln - kb)
                    m = p.mm(V(b.psA, boff + kb, [[2048, 128], [1, n]]),
                             V(qT.tensor, qt * 128, [[14336, 128], [1, 128]]),
                             V(kT.tensor, 2048 + c0k + kb,
                               [[14336, 128], [1, n]]),
                             start=True, stop=True,
                             waits=pwaits + [ev_k])
                wm = [m]
                if c0k <= klen - 128 < c0k + ln:
                    wm = [p.tt(V(b.psA, boff + (klen - 128 - c0k),
                                 [[2048, 128], [1, 128]]),
                               V(b.psA, boff + (klen - 128 - c0k),
                                 [[2048, 128], [1, 128]]),
                               mask_up, ALU.add, waits=[m])]
                c_nm = b.col(cb + ci)
                c_bi = b.col(cb + 2 + ci)
                c_su = b.col(cb + 4 + ci)
                psS = V(b.psA, boff, [[2048, 128], [1, ln]])
                t_nm = p.red(c_nm, psS, ALU.max, negate=True,
                             waits=wm + pwaits + (lnlast if h == 0 else []))
                t_bc = p.ts(c_bi, c_nm, RSQ, None, ALU.mult,
                            waits=[t_nm] + pwaits)
                t_e = p.act(atT(c0k, ln), psS, AF.Exp,
                            waits=[t_bc] + hw_prev, bias=c_bi, scale=RSQ,
                            accum=c_su)
                etoks.append(t_e)
                stats.append((c_nm, c_su, t_nm, t_e))
            if len(chunks) == 1:
                c_nmM, c_suM = stats[0][0], stats[0][1]
                t_mrg = [stats[0][2], stats[0][3]]
            else:
                c_nmM, c_suM = b.col(cb + 8), b.col(cb + 11)
                ca, cb2 = b.col(cb + 9), b.col(cb + 10)
                t0 = p.tt(c_nmM, stats[0][0], stats[1][0],
                          ALU.min, waits=[stats[0][2], stats[1][2]])
                t1 = p.tt(ca, c_nmM, stats[0][0], ALU.subtract, waits=[t0])
                t2 = p.tt(cb2, c_nmM, stats[1][0], ALU.subtract, waits=[t0])
                t3 = p.act(ca, ca, AF.Exp, waits=[t1, stats[0][3]], scale=RSQ)
                t4 = p.act(cb2, cb2, AF.Exp, waits=[t2, stats[1][3]],
                           scale=RSQ)
                t5 = p.tt(ca, ca, stats[0][1], ALU.mult, waits=[t3])
                t6 = p.tt(cb2, cb2, stats[1][1], ALU.mult, waits=[t4])
                t_mrg = [p.tt(c_suM, ca, cb2, ALU.add, waits=[t5, t6])]
            t_ln = p.act(b.col(cb + 6), c_suM, AF.Ln,
                         waits=t_mrg + ([prev_t[pp]] if prev_t[pp] else []))
            t_t = p.ts(b.col(cb + 7), b.col(cb + 6), 1.0 / RSQ, None,
                       ALU.mult, waits=[t_ln])
            t_rv = p.tt(b.cols[:, 32 + qt:33 + qt], b.col(cb + 7), c_nmM,
                        ALU.subtract,
                        waits=[t_t] + ([t_tr_prev] if t_tr_prev and qt == 0
                                       else []))
            prev_t[pp] = t_t
            prev_e = etoks[-1]
            if klen <= 1024:
                pair_free[pp] = etoks
            else:
                pair_free[0] = [etoks[0]]
                pair_free[1] = [etoks[1]]
        t_tr = p.trans(V(b.psB, 0, [[1024, 16], [1, 128]]),
                       b.cols[:, 32:48], b.eyef[:, :],
                       waits=[t_rv, evv[-1], prev_e])
        t_tr_prev = t_tr
        t_ev = p.copy("vector", b.rowsf[:, :],
                      V(b.psB, 0, [[1024, 16], [1, 128]]), waits=[t_tr])
        t_fl = p.dma("sync", b.rowflat,
                     V(b.rowsf, 0, [[128, 16], [1, 128]]),
                     waits=[t_ev] + ([t_lo_prev] if t_lo_prev else []),
                     sem="DR")
        t_ng = p.ts(b.rowflat, b.rowflat, -1.0, None, ALU.mult, waits=[t_fl])
        t_hi = p.copy("vector", V(b.rowbf, 0, [[4096, 1], [1, 2048]]),
                      b.rowflat, waits=[t_ng])
        t_lo = p.tt(V(b.rowbf, 2048, [[4096, 1], [1, 2048]]),
                    b.rowflat,
                    V(b.rowbf, 0, [[4096, 1], [1, 2048]]),
                    ALU.subtract, waits=[t_hi])
        t_lo_prev = t_lo
        av_started = [False] * 4
        half_free = [t_ev, evv[-1]]
        at_free = None
        last_av = None
        for kt in range(NT):
            k0 = kt * 128
            qb0 = k0 // 512
            t_z = None
            if k0 % 512:
                t_z = p.memset("vector", atT(qb0 * 512, k0 - qb0 * 512), 0.0,
                               waits=[at_free] if at_free else [])
            for qb in range(qb0, 4):
                q0 = qb * 512
                hf = qb % 2
                poff = hf * 512
                psS = V(b.psB, poff, [[1024, 128], [1, 512]])
                w0 = []
                if half_free[hf]:
                    w0.append(half_free[hf])
                m1 = p.mm(psS, V(kT.tensor, 2048 + k0,
                                 [[14336, 128], [1, 128]]),
                          V(qT.tensor, q0, [[14336, 128], [1, 512]]),
                          start=True, stop=False, waits=w0)
                p.mm(psS, ones1[0:1, 0:128],
                     V(b.rowbf, q0, [[4096, 1], [1, 512]]),
                     start=False, stop=False, waits=[t_hi])
                m3 = p.mm(psS, ones1[0:1, 0:128],
                          V(b.rowbf, 2048 + q0, [[4096, 1], [1, 512]]),
                          start=False, stop=True, waits=[t_lo])
                wmask = [m3]
                if q0 <= k0 < q0 + 512:
                    wmask = [p.tt(V(b.psB, poff + (k0 - q0),
                                    [[1024, 128], [1, 128]]),
                                  V(b.psB, poff + (k0 - q0),
                                    [[1024, 128], [1, 128]]),
                                  mask_low, ALU.add, waits=[m3])]
                v0 = max(q0, k0)
                if at_free:
                    wmask.append(at_free)
                et = p.act(atT(v0, q0 + 512 - v0),
                           V(b.psB, poff + (v0 - q0),
                             [[1024, 128], [1, 512 - (v0 - q0)]]),
                           AF.Exp, waits=wmask, scale=RSQ)
                half_free[hf] = et
                w = [et] + ([t_z] if t_z else [])
                last_av = p.mm(V(b.psA, q0, [[2048, 128], [1, 512]]),
                               vh(kt), atT(q0, 512),
                               start=(not av_started[qb]),
                               stop=(kt == (qb + 1) * 4 - 1),
                               waits=w + (pair_free[0] + pair_free[1]
                                          if kt == 0 else []))
                av_started[qb] = True
            at_free = last_av
        ev_av = p.copy("vector", V(b.slab2, h * 2048, [[32768, 128], [1, 2048]]),
                       b.psA[:, :], waits=[last_av])
        hw_prev = [last_av]
    p.barrier()

    t_wo = []
    wosem = ["DW", "DB", "DC0", "DC1"]
    for c in range(4):
        t_wo.append(p.dma(
            "gpsimd",
            V(b.slab1, c * 4 * 2048, [[32768, 128], [2048, 4], [1, 2048]]),
            V(io["l0_wo"], c * 4 * 262144,
              [[2048, 128], [262144, 4], [1, 2048]]),
            sem=wosem[c]))
    t_bo = p.dma("gpsimd", V(b.rowbf, 0, [[4096, 1], [1, 2048]]),
                 io["l0_bo"][0:1, :], sem="DC2")
    xt = V(b.slab3, 0, [[4096, 128], [1, 2048]])
    xo = V(b.slab3, 2048, [[4096, 128], [1, 2048]])
    prev = []
    for i in range(NT):
        tx = p.dma("sync", xt, io["x_in"][i * 128:(i + 1) * 128, :],
                   waits=prev, sem="DX")
        mt = []
        for fb in range(4):
            psq = V(b.psA, fb * 512, [[2048, 128], [1, 512]])
            for hc in range(NH):
                mt.append(p.mm(psq,
                               V(b.slab2, hc * 2048 + i * 128,
                                 [[32768, 128], [1, 128]]),
                               V(b.slab1, hc * 2048 + fb * 512,
                                 [[32768, 128], [1, 512]]),
                               start=(hc == 0), stop=False,
                               waits=[t_wo[hc // 4]] + prev))
            mt.append(p.mm(psq, ones1[0:1, 0:128],
                           V(b.rowbf, fb * 512, [[4096, 1], [1, 512]]),
                           start=False, stop=True, waits=[t_bo]))
        t_add = p.tt(xo, b.psA[:, :], xt, ALU.add, waits=[mt[-1], tx])
        t_st = p.dma("sync", io["x_out"][i * 128:(i + 1) * 128, :], xo,
                     waits=[t_add], sem="DS")
        prev = [t_add, t_st]
    p.barrier()


def ph_l0ffn(p, b, io):
    common = load_common(p, b, io)
    t_gw = p.dma("gpsimd", V(b.slab6, 0, [[6144, 128], [9, 16], [1, 9]]),
                 V(io["l0f_gws"], 0, [[9, 128], [1152, 16], [1, 9]]))
    p.barrier()
    xsT = (b.slab1, 0)
    hist = []
    for i in range(NT):
        pr = {}
        if hist:
            pr["ev"] = hist[-1]["ev"]
            pr["gate"] = list(hist[-1]["xt_readers"])
        if len(hist) >= 2:
            pr["gate"] = pr.get("gate", []) + hist[-2]["par_toks"]
        d, _ = emit_ln(p, b, io["x_in"], xsT, i, double_ln=False, prev=pr)
        hist.append(d)
    lnev = [h_["ev"] for h_ in hist]
    lnlast = [hist[14]["xt_readers"][-1], hist[15]["xt_readers"][-1]]
    lntt = [hist[14]["par_toks"][1], hist[15]["par_toks"][1]]
    xsTv = lambda dc, t0, n: V(b.slab1, dc * 2048 + t0, [[32768, 128], [1, n]])
    comb_tok = [None] * NT
    sg_tok = [None] * NT
    gate_free = [None, None]
    for i in range(NT):
        gpar0 = i % 2
        psg = V(b.psB, gpar0 * 512, [[1024, 128], [1, 9]])
        mt = []
        for dc in range(ND):
            mt.append(p.mm(psg, xsTv(dc, i * 128, 128),
                           V(b.slab6, dc * 9, [[6144, 128], [1, 9]]),
                           start=(dc == 0), stop=(dc == ND - 1),
                           waits=[t_gw, lnev[i]] + ([gate_free[gpar0]]
                                                    if gate_free[gpar0]
                                                    else [])))
        gpar = i % 2
        c20, c21, c22, c23, c24, c25, c26, c27 = [b.col(j)
                                                  for j in range(20 + gpar * 16,
                                                                 28 + gpar * 16)]
        z8 = b.cols[:, 232 + gpar * 64:240 + gpar * 64]
        eqb = b.cols[:, 240 + gpar * 64:248 + gpar * 64]
        ltb = b.cols[:, 248 + gpar * 64:256 + gpar * 64]
        ps8 = V(b.psB, gpar * 512, [[1024, 128], [1, 8]])
        t0 = p.red(c20, ps8, ALU.max, negate=True,
                   waits=[mt[-1]] + (lnlast if i < 2 else []))
        t1 = p.act(z8, ps8, AF.Exp, waits=[t0], bias=c20)
        t2 = p.act(b.cols[:, 192 + i:193 + i],
                   V(b.psB, gpar * 512 + 8, [[1024, 128], [1, 1]]),
                   AF.Sigmoid, waits=[mt[-1]])
        t3 = p.red(c21, z8, ALU.max, waits=[t1])
        t4 = p.ts(eqb, z8, c21, None, ALU.is_equal, waits=[t3])
        t5 = p.red(c22, eqb, ALU.add, waits=[t4])
        t6 = p.ts(ltb, z8, c21, None, ALU.is_lt, waits=[t3])
        t7 = p.tt(ltb, z8, ltb, ALU.mult, waits=[t6])
        t8 = p.red(c23, ltb, ALU.max, waits=[t7])
        t9 = p.ts(c24, c22, 2.0, None, ALU.is_ge, waits=[t5])
        t10 = p.ts(c25, c24, -1.0, 1.0, ALU.mult, ALU.add, waits=[t9])
        t11 = p.tt(c26, c24, c21, ALU.mult, waits=[t9])
        t12 = p.tt(c27, c25, c23, ALU.mult, waits=[t10, t8])
        t13 = p.tt(c26, c26, c27, ALU.add, waits=[t11, t12])
        t14 = p.tt(c27, c21, c26, ALU.add, waits=[t13])
        t15 = p.recip(c27, c27, waits=[t14])
        t16 = p.ts(eqb, z8, c26, None, ALU.is_ge, waits=[t13, t4])
        comb_tok[i] = p.stt(b.cols[:, 64 + i * 8:72 + i * 8], z8, c27, eqb,
                            ALU.mult, ALU.mult, waits=[t15, t16, t1])
        sg_tok[i] = t2
        gate_free[gpar0] = t2 if t2[1] > t1[1] else t1
    p.barrier()

    hv = lambda ic, t0, n: V(b.slab2, ic * 2048 + t0, [[32768, 128], [1, n]])
    for e in range(12):
        p.barrier()
        prev_mm = {}
        gu_free = {}
        it = 0
        for ic in range(11):
            par = ic % 2
            woff = par * 4096
            t_w = p.dma("gpsimd",
                        V(b.slab4, woff, [[14336, 128], [256, 16], [1, 256]]),
                        V(io["l0f_wgu"], (e * 11 + ic) * 2048 * 256,
                          [[256, 128], [32768, 16], [1, 256]]),
                        sem=("DW" if par == 0 else "DB"),
                        waits=prev_mm.get(par, [])
                        + (lntt if e == 0 and ic < 2 else []))
            for tb in range(4):
                pp = it % 2
                it += 1
                psg = V(b.psA, pp * 1024, [[2048, 128], [1, 512]])
                psu = V(b.psA, pp * 1024 + 512, [[2048, 128], [1, 512]])
                mw = [t_w, lnev[tb * 4 + 3]] + gu_free.get(pp, [])
                m1 = None
                for dc in range(ND):
                    m1 = p.mm(psg, V(b.slab4, woff + dc * 256,
                                     [[14336, 128], [1, 128]]),
                              xsTv(dc, tb * 512, 512),
                              start=(dc == 0), stop=(dc == ND - 1), waits=mw)
                for dc in range(ND):
                    m2 = p.mm(psu, V(b.slab4, woff + dc * 256 + 128,
                                     [[14336, 128], [1, 128]]),
                              xsTv(dc, tb * 512, 512),
                              start=(dc == 0), stop=(dc == ND - 1), waits=mw)
                sil = V(b.slab4, 8192 + pp * 512, [[14336, 128], [1, 512]])
                t_si = p.act(sil, psg, AF.Silu,
                             waits=[m1, m2] + gu_free.get(pp, [])[1:])
                t_h = p.tt(hv(ic, tb * 512, 512), sil,
                           psu, ALU.mult, waits=[t_si])
                gu_free[pp] = [t_si, t_h]
            prev_mm[par] = [m2]
        p.barrier()
        wd_last = (prev_mm.get(0, []) + prev_mm.get(1, [])
                   + gu_free.get(0, []) + gu_free.get(1, []))
        dn_seed = gu_free.get(0, [])
        stage_dma = [None] * 4
        for fh in range(2):
            t_wd = []
            for wh2 in range(2):
                n_ic = 6 if wh2 == 0 else 5
                t_wd.append(p.dma(
                    "gpsimd",
                    V(b.slab4, wh2 * 6144,
                      [[14336, 128], [1024, n_ic], [1, 1024]]),
                    V(io["l0f_wd"], (e * 2 + fh) * 1408 * 1024 + wh2 * 6 * 131072,
                      [[1024, 128], [131072, n_ic], [1, 1024]]),
                    sem=("DW" if wh2 == 0 else "DB"), waits=wd_last))
            dn_free = {1: dn_seed}
            for t in range(NT):
                pp = t % 2
                base = 0 if pp == 0 else 1024
                psl = (b.psB if pp == 0 else b.psA)
                m = None
                for ic in range(11):
                    w = ([t_wd[0] if ic < 6 else t_wd[1]]
                         + dn_free.get(pp, []))
                    m = p.mm(V(psl, 0, [[psl.shape[1], 128], [1, 512]]),
                             hv(ic, t * 128, 128),
                             V(b.slab4, ic * 1024, [[14336, 128], [1, 512]]),
                             start=(ic == 0), stop=(ic == 10), waits=w)
                    m = p.mm(V(psl, 512, [[psl.shape[1], 128], [1, 512]]),
                             hv(ic, t * 128, 128),
                             V(b.slab4, ic * 1024 + 512,
                               [[14336, 128], [1, 512]]),
                             start=(ic == 0), stop=(ic == 10), waits=w)
                scale = (b.cols[:, 64 + t * 8 + e:65 + t * 8 + e] if e < 8
                         else b.cols[:, 192 + t:193 + t])
                sg = t % 4
                wsc = [m, comb_tok[t] if e < 8 else sg_tok[t]]
                if stage_dma[sg]:
                    wsc.append(stage_dma[sg])
                t_sc = p.act(V(b.slab3, sg * 1024, [[4096, 128], [1, 1024]]),
                             V(psl, 0, [[psl.shape[1], 128], [1, 1024]]),
                             AF.Identity, waits=wsc, scale=scale)
                dn_free[pp] = [t_sc]
                stage_dma[sg] = p.dma(
                    "gpsimd",
                    V(io["moe"], t * 128 * 2048 + fh * 1024,
                      [[2048, 128], [1, 1024]]),
                    V(b.slab3, sg * 1024, [[4096, 128], [1, 1024]]),
                    waits=[t_sc], sem="DA%d" % sg,
                    accum_op=(ALU.add if e > 0 else ALU.bypass))
            wd_last = [m]
        p.barrier()
    p.barrier()
    xt = V(b.slab3, 0, [[4096, 128], [1, 2048]])
    mo = V(b.slab3, 2048, [[4096, 128], [1, 2048]])
    prev = []
    for i in range(NT):
        tx = p.dma("sync", xt, io["x_in"][i * 128:(i + 1) * 128, :],
                   waits=prev, sem="DX")
        tm = p.dma("sync", mo, io["moe"][i * 128:(i + 1) * 128, :],
                   waits=prev, sem="DR")
        ta = p.tt(xt, xt, mo, ALU.add, waits=[tx, tm])
        ts_ = p.dma("sync", io["x_out"][i * 128:(i + 1) * 128, :], xt,
                    waits=[ta], sem="DS")
        prev = [ta, ts_]
    p.barrier()


def ph_l1attn(p, b, io, dbg=False):
    common = load_common(p, b, io)
    t_be = p.dma("sync", b.slab3[:, 0:16], io["l1_betas"][:, :])
    p.act(b.cols[:, 28:44], b.slab3[:, 0:16], AF.Sigmoid, waits=[t_be])
    p.barrier()
    xsT = (b.slab1, 0)
    hist = []
    for i in range(NT):
        pr = {}
        if hist:
            pr["ev"] = hist[-1]["ev"]
            pr["gate"] = list(hist[-1]["xt_readers"])
        if len(hist) >= 2:
            pr["gate"] = pr.get("gate", []) + hist[-2]["par_toks"]
        d, _ = emit_ln(p, b, io["x_in"], xsT, i, double_ln=False, prev=pr)
        hist.append(d)
    p.barrier()
    xsTv = lambda dc, t0, n: V(b.slab1, dc * 2048 + t0, [[32768, 128], [1, n]])
    qT = lambda c0, n: V(b.slab4, c0, [[14336, 128], [1, n]])
    kT = lambda c0, n: V(b.slab4, 2048 + c0, [[14336, 128], [1, n]])
    vh = lambda kt: V(b.slab4, 4096 + kt * 128, [[14336, 128], [1, 128]])
    sqT = lambda c0, n: V(b.slab4, 6144 + c0, [[14336, 128], [1, n]])
    skT = lambda c0, n: V(b.slab4, 8192 + c0, [[14336, 128], [1, n]])
    sktok = lambda ss: V(b.slab4, 10240 + ss * 128, [[14336, 128], [1, 128]])
    AT = lambda ks, c0, n: V(b.slab4, 12288 + ks * 512 + c0,
                             [[14336, 128], [1, n]])
    dlt = lambda ss: V(b.slab6, ss * 128, [[6144, 128], [1, 128]])
    membf = V(b.slab6, 512, [[6144, 128], [1, 128]])
    zbf = V(b.slab6, 640, [[6144, 128], [1, 1]])
    mem = b.memf[:, 0:128]
    zz = b.memf[:, 128:129]
    whd = lambda dc, c0, n: V(b.slab6, dc * 384 + c0, [[6144, 128], [1, n]])
    scr1 = V(b.slab3, 0, [[4096, 128], [1, 512]])
    scr2 = V(b.slab3, 512, [[4096, 128], [1, 512]])
    rflat = V(b.slab3, 2048, [[4096, 1], [1, 512]])
    rden = V(b.slab3, 2560, [[4096, 1], [1, 512]])
    hi_r = lambda c0, n: V(b.rowbf, c0, [[4096, 1], [1, n]])
    lo_r = lambda c0, n: V(b.rowbf, 2048 + c0, [[4096, 1], [1, n]])
    rdenb = V(b.rowbf, 1024, [[4096, 1], [1, 512]])
    ones1 = V(b.onesbf, 0, [[512, 1], [1, 512]])
    mask_up = V(b.masks, 0, [[256, 128], [1, 128]])
    mask_low = V(b.masks, 128, [[256, 128], [1, 128]])
    c_zs = b.col(60)
    t_lo_p3 = None
    t_tr_prev = None

    for h in range(NH):
        t_w = p.dma("gpsimd",
                    V(b.slab6, 0, [[6144, 128], [384, 16], [1, 384]]),
                    V(io["l1_wqkv"], h * 2048 * 384,
                      [[384, 128], [49152, 16], [1, 384]]), sem="DW")
        beta = b.cols[:, 28 + h:29 + h]
        ev_qk = None
        for (c0d, c0w) in ((0, 0), (2048, 128)):
            mt = []
            for tb in range(4):
                psq = V(b.psA, tb * 512, [[2048, 128], [1, 512]])
                for dc in range(ND):
                    mt.append(p.mm(psq, whd(dc, c0w, 128),
                                   xsTv(dc, tb * 512, 512),
                                   start=(dc == 0), stop=(dc == ND - 1),
                                   waits=[t_w] + ([ev_qk] if ev_qk else [])))
            ev_qk = p.copy("vector", V(b.slab4, c0d, [[14336, 128], [1, 2048]]),
                           b.psA[:, :], waits=[mt[-1]])
        ev_k = ev_qk
        evv = []
        for kt in range(NT):
            psv = V(b.psB, (kt % 2) * 512, [[1024, 128], [1, 128]])
            w = [evv[-2]] if kt >= 2 else []
            m = None
            for dc in range(ND):
                m = p.mm(psv, xsTv(dc, kt * 128, 128), whd(dc, 256, 128),
                         start=(dc == 0), stop=(dc == ND - 1),
                         waits=w + [t_w])
            evv.append(p.copy("vector", vh(kt), psv, waits=[m]))
        ms1 = p.memset("vector", mem, 0.0)
        ms2 = p.memset("vector", zz, 1.0 / 128.0)
        ms3 = p.memset("vector", membf, 0.0)
        ms4 = p.memset("vector", zbf, 1.0 / 128.0)
        w1 = V(b.slab3, 0, [[4096, 128], [1, 2048]])
        w2 = V(b.slab3, 2048, [[4096, 128], [1, 2048]])
        a1 = p.ts(w1, qT(0, 2048), 0.0, None, ALU.min, waits=[ev_k])
        a2 = p.act(w2, w1, AF.Exp, waits=[a1])
        a3 = p.ts(w1, qT(0, 2048), 0.0, None, ALU.max, waits=[a2])
        tqs = p.tt(sqT(0, 2048), w2, w1, ALU.add, waits=[a3, a2])
        a4 = p.ts(w1, kT(0, 2048), 0.0, None, ALU.min, waits=[tqs])
        a5 = p.act(w2, w1, AF.Exp, waits=[a4])
        a6 = p.ts(w1, kT(0, 2048), 0.0, None, ALU.max, waits=[a5])
        tks = p.tt(skT(0, 2048), w2, w1, ALU.add, waits=[a6, a5])
        ttk = None
        for ss in range(NT):
            ttk = p.trans(V(b.psT, ss * 128, [[2048, 128], [1, 128]]),
                          skT(ss * 128, 128), b.eyebf[:, :],
                          waits=[tks, evv[-1]])
        t_sk = p.copy("vector", V(b.slab4, 10240, [[14336, 128], [1, 2048]]),
                      V(b.psT, 0, [[2048, 128], [1, 2048]]), waits=[ttk])
        at6 = V(b.psT, 0, [[2048, 128], [1, 1024]]).bitcast(F32)
        _t6 = V(b.psT, 0, [[2048, 128], [1, 2048]]).bitcast(F32).tensor
        at6s = bass.AP(_t6, 0, [[1024, 128], [1, 16], [16, 32]])
        mu7 = V(b.psT, 1024, [[2048, 128], [1, 256]]).bitcast(F32)

        def emit_scores(s):
            s0_ = s * 512
            toks = []
            for qs in range(4):
                kl = (qs + 1) * 128
                psS = V(b.psA, qs * 512, [[2048, 128], [1, kl]])
                m1 = p.mm(psS, qT(s0_ + qs * 128, 128), kT(s0_, kl),
                          start=True, stop=True,
                          waits=sc_free[qs] + [ev_k])
                tmk = p.tt(V(b.psA, qs * 512 + kl - 128,
                             [[2048, 128], [1, 128]]),
                           V(b.psA, qs * 512 + kl - 128,
                             [[2048, 128], [1, 128]]),
                           mask_up, ALU.add, waits=[m1])
                t0 = p.red(b.col(0 + qs), psS, ALU.max, negate=True,
                           waits=[tmk])
                t1 = p.ts(b.col(4 + qs), b.col(0 + qs), RSQ, None, ALU.mult,
                          waits=[t0] + sc_free[qs])
                t2 = p.act(AT(qs, 0, kl), psS, AF.Exp,
                           waits=[t1] + at_free, bias=b.col(4 + qs),
                           scale=RSQ, accum=b.col(8 + qs))
                t3 = p.act(b.col(12 + qs), b.col(8 + qs), AF.Ln, waits=[t2])
                t4 = p.ts(b.col(16 + qs), b.col(12 + qs), 1.0 / RSQ, None,
                          ALU.mult, waits=[t3])
                t5 = p.tt(b.cols[:, 20 + qs:21 + qs], b.col(16 + qs),
                          b.col(0 + qs), ALU.subtract,
                          waits=[t4] + rc_free)
                toks.append((t2, t5))
            return toks

        sc_free = [[] for _ in range(4)]
        at_free = []
        rc_free = [t_tr_prev] if t_tr_prev else []
        b4_free = [evv[-2]] if len(evv) >= 2 else []
        b5_free = [evv[-1]]
        p6_free = [t_sk]
        p7_free = [t_sk]
        st_prev = []
        stw_prev = []

        def emit_row(s, sc_toks):
            rp = s % 2
            rfl = V(b.slab3, 2048 + rp * 1024, [[4096, 1], [1, 512]])
            t_tr = p.trans(V(b.psA, 0, [[2048, 4], [1, 128]]),
                           b.cols[:, 20:24], b.eyef[:, :],
                           waits=[t for _, t in sc_toks]
                           + [sc_toks[0][0]] + rc_free)
            rta = (V(b.rowsf, 0, [[128, 4], [1, 128]]) if rp == 0
                   else V(b.memf, 1024, [[2080, 4], [1, 128]]))
            t_ev = p.copy("vector", rta,
                          V(b.psA, 0, [[2048, 4], [1, 128]]), waits=[t_tr])
            t_fl = p.dma("sync", rfl, rta, waits=[t_ev], sem="DR")
            t_ng = p.ts(rfl, rfl, -1.0, None, ALU.mult, waits=[t_fl])
            t_hi = p.copy("vector", V(b.rowbf, rp * 512, [[4096, 1], [1, 512]]),
                          rfl, waits=[t_ng] + stw_prev)
            t_lo = p.tt(V(b.rowbf, 2048 + rp * 512, [[4096, 1], [1, 512]]),
                        rfl, V(b.rowbf, rp * 512, [[4096, 1], [1, 512]]),
                        ALU.subtract, waits=[t_hi])
            return (t_hi, t_lo, t_ev, rp)

        sc_toks = emit_scores(0)
        rowtoks = emit_row(0, sc_toks)
        for seg in range(4):
            s0 = seg * 512
            t_hi, t_lo, t_ev_row, rp = rowtoks
            half_free = [b4_free[-1] if b4_free else None,
                         b5_free[-1] if b5_free else None]
            prev_av = None
            exp_last = sc_toks[-1][0]
            stw = []
            for ks in range(4):
                k0s = ks * 128
                hf = ks % 2
                psS = V(b.psB, hf * 512 + k0s, [[1024, 128], [1, 512 - k0s]])
                w0 = [tks] + ([half_free[hf]] if half_free[hf] else [])
                m1 = p.mm(psS, kT(s0 + k0s, 128), qT(s0 + k0s, 512 - k0s),
                          start=True, stop=False, waits=w0)
                p.mm(psS, ones1[0:1, 0:128],
                     V(b.rowbf, rp * 512 + k0s, [[4096, 1], [1, 512 - k0s]]),
                     start=False, stop=False, waits=[t_hi])
                m3 = p.mm(psS, ones1[0:1, 0:128],
                          V(b.rowbf, 2048 + rp * 512 + k0s,
                            [[4096, 1], [1, 512 - k0s]]),
                          start=False, stop=True, waits=[t_lo])
                stw.append(m3)
                tmk = p.tt(V(b.psB, hf * 512 + k0s, [[1024, 128], [1, 128]]),
                           V(b.psB, hf * 512 + k0s, [[1024, 128], [1, 128]]),
                           mask_low, ALU.add, waits=[m3])
                et = p.act(AT(ks, k0s, 512 - k0s), psS, AF.Exp,
                           waits=[tmk, exp_last], scale=RSQ)
                half_free[hf] = et
                prev_av = p.mm(V(b.psT, 2 * k0s,
                                 [[2048, 128], [1, 2 * (512 - k0s)]]
                                 ).bitcast(F32),
                               vh(seg * 4 + ks), AT(ks, k0s, 512 - k0s),
                               start=(ks == 0), stop=(ks == 3),
                               waits=[et, evv[-1]] + (p6_free if ks == 0
                                                      else []))
            at_free = [prev_av]
            stw_prev = [stw[-1]]
            sc_free = [[t2] for t2, _ in sc_toks]
            sc_free[0].append(rowtoks[2])
            rc_free = []
            if seg < 3:
                sc_toks = emit_scores(seg + 1)
                rowtoks = emit_row(seg + 1, sc_toks)
            msw = [ms3, ms4] if seg == 0 else st_prev
            mq = p.mm(V(b.psB, 0, [[1024, 128], [1, 512]]), membf,
                      sqT(s0, 512), start=True, stop=True,
                      waits=[half_free[0], tqs] + msw)
            md = p.mm(V(b.psB, 512, [[1024, 1], [1, 512]]), zbf, sqT(s0, 512),
                      start=True, stop=True,
                      waits=[half_free[1], tqs] + msw)
            t_rd = p.recip(rden, V(b.psB, 512, [[1024, 1], [1, 512]]),
                           waits=[md])
            t_rb = p.copy("vector", rdenb, rden, waits=[t_rd])
            mb = p.mm(V(b.psB, 512, [[1024, 128], [1, 512]]),
                      ones1[0:1, 0:128], rdenb, start=True, stop=True,
                      waits=[t_rb])
            t_nq = p.copy("vector", scr1, V(b.psB, 0, [[1024, 128], [1, 512]]),
                          waits=[mq])
            t_am = p.tt(scr2, scr1, V(b.psB, 512, [[1024, 128], [1, 512]]),
                        ALU.mult, waits=[t_nq, mb])
            t_s = p.tt(scr1, scr2, at6, ALU.subtract, waits=[t_am, prev_av])
            t_att = p.stt(V(b.slab2, (seg * 4 + h // 4) * 2048 + (h % 4) * 32,
                            [[32768, 128], [128, 16], [1, 32]]),
                          V(b.slab3, 0, [[4096, 128], [1, 16], [16, 32]]),
                          beta, at6s, ALU.mult, ALU.add, waits=[t_s])
            p6_free = [t_att]
            tdl = []
            t_r = None
            for ss in range(4):
                nk = p.mm(V(b.psB, ss * 128, [[1024, 128], [1, 128]]),
                          skT(s0 + ss * 128, 128), membf, start=True,
                          stop=True, waits=[t_am, tks])
                dk_ = p.mm(V(b.psB, 512 + ss, [[1024, 128], [1, 1]]),
                           skT(s0 + ss * 128, 128), zbf, start=True, stop=True,
                           waits=[t_am, tks])
                t_r = p.recip(b.col(56 + ss),
                              V(b.psB, 512 + ss, [[1024, 128], [1, 1]]),
                              waits=[dk_])
                t_r2 = p.ts(b.col(56 + ss), b.col(56 + ss), -1.0, None,
                            ALU.mult, waits=[t_r])
                tdl.append(p.stt(dlt(ss),
                                 V(b.psB, ss * 128, [[1024, 128], [1, 128]]),
                                 b.col(56 + ss), vh(seg * 4 + ss),
                                 ALU.mult, ALU.add, waits=[t_r2, nk]))
            b4_free = [tdl[-1]]
            b5_free = [t_r]
            mu = None
            for ss in range(4):
                mu = p.mm(mu7, sktok(seg * 4 + ss), dlt(ss),
                          start=(ss == 0), stop=(ss == 3),
                          waits=[t_sk, tdl[ss]] + (p7_free if ss == 0 else []))
            t_m = p.tt(mem, mem, mu7, ALU.add,
                       waits=[mu] + ([ms1] if seg == 0 else []))
            t_mb = p.copy("vector", membf, mem, waits=[t_m])
            p7_free = [t_m]
            t_z0 = p.red(c_zs, skT(s0, 512), ALU.add,
                         waits=[tks] + ([st_prev[-1]] if st_prev else []))
            t_z1 = p.tt(zz, zz, c_zs, ALU.add,
                        waits=[t_z0] + ([ms2] if seg == 0 else []))
            t_zb = p.copy("vector", zbf, zz, waits=[t_z1])
            st_prev = [t_mb, t_zb]
        p.barrier()
        if dbg and h == NH - 1:
            xt_ = V(b.slab3, 0, [[4096, 128], [1, 2048]])
            for r in range(16):
                t_c = p.copy("vector", xt_,
                             V(b.slab2, r * 2048, [[32768, 128], [1, 2048]]))
                p.dma("sync", io["x_out"][r * 128:(r + 1) * 128, :], xt_,
                      waits=[t_c], sem="DS")
                p.barrier()
            return
    t_wo = []
    wosem3 = ["DW", "DB", "DC0", "DC1"]
    for c in range(4):
        t_wo.append(p.dma(
            "gpsimd",
            V(b.slab1, c * 4 * 2048, [[32768, 128], [2048, 4], [1, 2048]]),
            V(io["l1_wo"], c * 4 * 262144,
              [[2048, 128], [262144, 4], [1, 2048]]),
            sem=wosem3[c]))
    xt = V(b.slab3, 0, [[4096, 128], [1, 2048]])
    xo = V(b.slab3, 2048, [[4096, 128], [1, 2048]])
    prev = []
    for i in range(NT):
        seg, gg = i // 4, i % 4
        tx = p.dma("sync", xt, io["x_in"][i * 128:(i + 1) * 128, :],
                   waits=prev, sem="DX")
        mt = None
        for fb in range(4):
            psq = V(b.psA, fb * 512, [[2048, 128], [1, 512]])
            for cc in range(NH):
                lhs = V(b.slab2, i * 2048 + cc * 128,
                        [[32768, 128], [1, 128]])
                mt = p.mm(psq, lhs,
                          V(b.slab1, cc * 2048 + fb * 512,
                            [[32768, 128], [1, 512]]),
                          start=(cc == 0), stop=(cc == NH - 1),
                          waits=[t_wo[cc // 4]] + prev)
        t_add = p.tt(xo, b.psA[:, :], xt, ALU.add, waits=[mt, tx])
        t_st = p.dma("sync", io["x_out"][i * 128:(i + 1) * 128, :], xo,
                     waits=[t_add], sem="DS")
        prev = [t_add, t_st]
    p.barrier()



def ph_l1ffn(p, b, io):
    common = load_common(p, b, io)
    s6 = lambda off, dims: V(b.slab6, off, dims)
    xsTv = lambda dc, t0, n: V(b.slab1, dc * 2048 + t0, [[32768, 128], [1, n]])
    p.barrier()
    xsT = (b.slab1, 0)
    hist = []
    for i in range(NT):
        pr = {}
        if hist:
            pr["ev"] = hist[-1]["ev"]
            pr["gate"] = list(hist[-1]["xt_readers"])
        if len(hist) >= 2:
            pr["gate"] = pr.get("gate", []) + hist[-2]["par_toks"]
        d, _ = emit_ln(p, b, io["x_in"], xsT, i, double_ln=False, prev=pr)
        hist.append(d)
    p.barrier()
    t_gw = p.dma("gpsimd", s6(4096, [[6144, 128], [8, 16], [1, 8]]),
                 V(io["l1f_wgt"], 0, [[8, 128], [1024, 16], [1, 8]]), sem="DW")
    gate_free = [None, None]
    for i in range(NT):
        gpar = i % 2
        psg = V(b.psB, gpar * 512, [[1024, 128], [1, 8]])
        mt = []
        for dc in range(ND):
            mt.append(p.mm(psg, xsTv(dc, i * 128, 128),
                           s6(4096 + dc * 8, [[6144, 128], [1, 8]]),
                           start=(dc == 0), stop=(dc == ND - 1),
                           waits=[t_gw] + ([gate_free[gpar]]
                                           if gate_free[gpar] else [])))
        cgb = 20 if gpar == 0 else 36
        c20, c21, c22, c23, c24, c25, c26, c27 = [b.col(j)
                                                  for j in range(cgb, cgb + 8)]
        z8 = b.cols[:, 360:368] if gpar == 0 else b.cols[:, 44:52]
        eqb = b.cols[:, 368:376] if gpar == 0 else b.cols[:, 52:60]
        ltb = b.cols[:, 376:384] if gpar == 0 else b.cols[:, 352:360]
        t0 = p.red(c20, psg, ALU.max, negate=True, waits=[mt[-1]])
        t1 = p.act(z8, psg, AF.Exp, waits=[t0], bias=c20)
        t3 = p.red(c21, z8, ALU.max, waits=[t1])
        t4 = p.ts(eqb, z8, c21, None, ALU.is_equal, waits=[t3])
        t5 = p.red(c22, eqb, ALU.add, waits=[t4])
        t6 = p.ts(ltb, z8, c21, None, ALU.is_lt, waits=[t3])
        t7 = p.tt(ltb, z8, ltb, ALU.mult, waits=[t6])
        t8 = p.red(c23, ltb, ALU.max, waits=[t7])
        t9 = p.ts(c24, c22, 2.0, None, ALU.is_ge, waits=[t5])
        t10 = p.ts(c25, c24, -1.0, 1.0, ALU.mult, ALU.add, waits=[t9])
        t11 = p.tt(c26, c24, c21, ALU.mult, waits=[t9])
        t12 = p.tt(c27, c25, c23, ALU.mult, waits=[t10, t8])
        t13 = p.tt(c26, c26, c27, ALU.add, waits=[t11, t12])
        t14 = p.tt(c27, c21, c26, ALU.add, waits=[t13])
        t15 = p.recip(c27, c27, waits=[t14])
        ta = p.tt(b.cols[:, 64 + i:65 + i], c21, c27, ALU.mult, waits=[t15])
        tb_ = p.tt(b.cols[:, 80 + i:81 + i], c26, c27, ALU.mult, waits=[t15])
        te0 = p.ts(b.cols[:, 96 + i * 8:104 + i * 8], z8, c21, None,
                   ALU.is_ge, waits=[t13, t1])
        te1 = p.ts(eqb, z8, c26, None, ALU.is_ge, waits=[t13])
        p.tt(b.cols[:, 224 + i * 8:232 + i * 8],
             eqb, b.cols[:, 96 + i * 8:104 + i * 8], ALU.subtract,
             waits=[te1, te0])
        gate_free[gpar] = t1
    p.barrier()
    lastmm = []
    aa_free = [None, None]
    for (name, dst) in (("l1f_a1t", 0), ("l1f_a3t", 4096)):
        t_a = p.dma("gpsimd", s6(0, [[6144, 128], [256, 16], [1, 256]]),
                    V(io[name], 0, [[256, 128], [32768, 16], [1, 256]]),
                    sem="DW", waits=lastmm)
        for i in range(NT):
            apar = i % 2
            psq = V(b.psB, apar * 512, [[1024, 128], [1, 256]])
            m = None
            for dc in range(ND):
                m = p.mm(psq, xsTv(dc, i * 128, 128),
                         s6(dc * 256, [[6144, 128], [1, 256]]),
                         start=(dc == 0), stop=(dc == ND - 1),
                         waits=[t_a] + ([aa_free[apar]] if aa_free[apar]
                                        else []))
            aa_free[apar] = p.copy(
                "vector", V(b.slab2, dst + i * 256, [[32768, 128], [1, 256]]),
                psq, waits=[m])
        lastmm = [m]
    p.barrier()
    prev_mm = {}
    prev_st = None
    bprev = {}
    for (name, dstio) in (("l1f_w1b", "base1"), ("l1f_w3b", "base3")):
        stg = V(b.slab4, 8192, [[14336, 128], [1, 2048]])
        for fc in range(56):
            par = fc % 2
            woff = par * 4096
            t_w = p.dma("gpsimd",
                        V(b.slab4, woff, [[14336, 128], [128, 16], [1, 128]]),
                        V(io[name], fc * 262144,
                          [[128, 128], [16384, 16], [1, 128]]),
                        sem=("DW" if par == 0 else "DB"),
                        waits=prev_mm.get(par, []))
            evs = []
            for tb in range(4):
                bpar = tb % 2
                psq = V(b.psB, bpar * 512, [[1024, 128], [1, 512]])
                bw = [t_w]
                if len(evs) >= 2:
                    bw.append(evs[-2])
                elif bprev.get(bpar):
                    bw.append(bprev[bpar])
                if tb == 0 and prev_st:
                    bw.append(prev_st)
                m = None
                for dc in range(ND):
                    m = p.mm(psq, V(b.slab4, woff + dc * 128,
                                    [[14336, 128], [1, 128]]),
                             xsTv(dc, tb * 512, 512),
                             start=(dc == 0), stop=(dc == ND - 1), waits=bw)
                evs.append(p.copy("vector",
                                  V(b.slab4, 8192 + tb * 512,
                                    [[14336, 128], [1, 512]]),
                                  psq, waits=[m, prev_st] if prev_st else [m]))
            bprev = {0: evs[-2], 1: evs[-1]}
            prev_mm[par] = [m]
            prev_st = p.dma("sync", V(io[dstio], fc * 2048,
                                      [[114688, 128], [1, 2048]]),
                            stg, waits=[evs[-1]], sem="DS")
    p.barrier()
    a2ps = lambda tl, n: V(b.psA, tl * 512, [[2048, 128], [1, n]])
    for k in range(2):
        p.barrier()
        emoff = 96 + 128 * k
        rwoff = 64 + 16 * k
        mk_ev = None
        for (asrc, mtmp, mdst) in ((0, 8192, 12288), (4096, 8192, 16384)):
            for i in range(NT):
                emask = V(b.cols, emoff + i * 8, [[384, 128], [1, 8], [0, 32]])
                a_ap = V(b.slab2, asrc + i * 256,
                         [[32768, 128], [32, 8], [1, 32]])
                t_m = p.tt(V(b.slab2, mtmp + i * 256,
                             [[32768, 128], [32, 8], [1, 32]]),
                           a_ap, emask, ALU.mult, waits=[])
                for ec in range(2):
                    t_tr = p.trans(V(b.psT, ec * 128, [[2048, 128], [1, 128]]),
                                   V(b.slab2, mtmp + i * 256 + ec * 128,
                                     [[32768, 128], [1, 128]]),
                                   b.eyebf[:, :],
                                   waits=[t_m] + ([mk_ev] if mk_ev else []))
                mk_ev = p.copy("vector",
                               V(b.slab2, mdst + i * 128,
                                 [[32768, 128], [2048, 2], [1, 128]]),
                               V(b.psT, 0, [[2048, 128], [128, 2], [1, 128]]),
                               waits=[t_tr])
        p.barrier()
        for tq in range(4):
            p.barrier()
            prev_g = {}
            x2free = {}
            a2last = None
            for fcg in range(14):
                par = fcg % 2
                wv = prev_g.get(par, [])
                t_b1 = p.dma("gpsimd",
                             V(b.slab4, par * 4096, [[14336, 128], [512, 4], [1, 512]]),
                             V(io["base1"], fcg * 4 * 2048 + tq * 512,
                               [[114688, 128], [2048, 4], [1, 512]]),
                             sem=("DW" if par == 0 else "DB"), waits=wv)
                t_b3 = p.dma("gpsimd",
                             V(b.slab4, par * 4096 + 2048,
                               [[14336, 128], [512, 4], [1, 512]]),
                             V(io["base3"], fcg * 4 * 2048 + tq * 512,
                               [[114688, 128], [2048, 4], [1, 512]]),
                             sem=("DC0" if par == 0 else "DC1"), waits=wv)
                t_bb = p.dma("gpsimd",
                             s6(par * 3072, [[6144, 128], [768, 4], [1, 768]]),
                             V(io["l1f_pack"], fcg * 4 * 98304,
                               [[768, 128], [98304, 4], [1, 768]]),
                             sem=("DC2" if par == 0 else "DA0"), waits=wv)
                t_b3b = t_a2 = t_bb
                lastg = []
                for fl in range(4):
                    fc = fcg * 4 + fl
                    xpar = fc % 2
                    if xpar == 0:
                        ps1 = V(b.psB, 0, [[1024, 128], [1, 512]])
                        ps3 = V(b.psB, 512, [[1024, 128], [1, 512]])
                    else:
                        ps1 = V(b.psT, 0, [[2048, 128], [1, 1024]]
                                ).bitcast(F32)
                        ps3 = V(b.psT, 1024, [[2048, 128], [1, 1024]]
                                ).bitcast(F32)
                    w0 = [t_b1, t_b3, t_bb] + x2free.get(xpar, [])
                    m = p.mm(ps1, b.eyebf[:, :],
                             V(b.slab4, par * 4096 + fl * 512,
                               [[14336, 128], [1, 512]]),
                             start=True, stop=False, waits=w0)
                    for ec in range(2):
                        m = p.mm(ps1, s6(par * 3072 + fl * 768 + ec * 128,
                                         [[6144, 128], [1, 128]]),
                                 V(b.slab2, 12288 + ec * 2048 + tq * 512,
                                   [[32768, 128], [1, 512]]),
                                 start=False, stop=(ec == 1), waits=[])
                    m3 = p.mm(ps3, b.eyebf[:, :],
                              V(b.slab4, par * 4096 + 2048 + fl * 512,
                                [[14336, 128], [1, 512]]),
                              start=True, stop=False, waits=w0)
                    for ec in range(2):
                        m3 = p.mm(ps3, s6(par * 3072 + fl * 768 + 256 + ec * 128,
                                          [[6144, 128], [1, 128]]),
                                  V(b.slab2, 16384 + ec * 2048 + tq * 512,
                                    [[32768, 128], [1, 512]]),
                                  start=False, stop=(ec == 1), waits=[])
                    sil = V(b.slab4, 12544 + xpar * 512,
                            [[14336, 128], [1, 512]])
                    t_si = p.act(sil, ps1, AF.Silu,
                                 waits=[m] + x2free.get(xpar, [])[1:])
                    t_x2 = p.tt(V(b.slab1, fc * 512, [[32768, 128], [1, 512]]),
                                sil, ps3, ALU.mult, waits=[t_si, m3])
                    x2free[xpar] = [t_si, t_x2]
                    for tl in range(4):
                        a2last = p.mm(a2ps(tl, 256),
                                      V(b.slab1, fc * 512 + tl * 128,
                                        [[32768, 128], [1, 128]]),
                                      s6(par * 3072 + fl * 768 + 512,
                                         [[6144, 128], [1, 256]]),
                                      start=(fc == 0), stop=(fc == 55),
                                      waits=[t_x2, t_bb])
                    lastg.append(a2last)
                prev_g[par] = [lastg[-1]]
            p.barrier()
            m2evs = []
            m2tmp_free = None
            for tl in range(4):
                tg = tq * 4 + tl
                emask = V(b.cols, emoff + tg * 8, [[384, 128], [1, 8], [0, 32]])
                rw = b.cols[:, rwoff + tg:rwoff + tg + 1]
                t_m2 = p.stt(V(b.slab4, 13568, [[14336, 128], [32, 8], [1, 32]]),
                             V(b.psA, tl * 512, [[2048, 128], [32, 8], [1, 32]]),
                             rw, emask, ALU.mult, ALU.mult,
                             waits=[a2last] + ([m2tmp_free] if m2tmp_free
                                               else []))
                for ec in range(2):
                    t_tr = p.trans(V(b.psT, ec * 128, [[2048, 128], [1, 128]]),
                                   V(b.slab4, 13568 + ec * 128,
                                     [[14336, 128], [1, 128]]),
                                   b.eyebf[:, :],
                                   waits=[t_m2] + (x2free.get(1, [])
                                                   if tl == 0 else [])
                                   + (m2evs[-1:] if m2evs else []))
                m2tmp_free = t_tr
                m2evs.append(p.copy(
                    "vector",
                    V(b.slab4, 8192 + tg * 128,
                      [[14336, 128], [2048, 2], [1, 128]]),
                    V(b.psT, 0, [[2048, 128], [128, 2], [1, 128]]),
                    waits=[t_tr]))
            t_b2 = p.dma("gpsimd", s6(0, [[6144, 128], [2048, 2], [1, 2048]]),
                         V(io["l1f_b2f"], 0,
                           [[2048, 128], [262144, 2], [1, 2048]]), sem="DA1",
                         waits=[a2last])
            psd = [V(b.psA, 0, [[2048, 128], [1, 1024]]),
                   V(b.psA, 1024, [[2048, 128], [1, 1024]]),
                   V(b.psB, 0, [[1024, 128], [1, 1024]]),
                   None]
            psd3 = V(b.psT, 0, [[2048, 128], [1, 2048]]).bitcast(F32)
            stage_dma = [None] * 4
            ev_prev = list(m2evs)
            for wh in range(2):
                prev_w = {}
                mlast = [None] * 4
                for fg in range(14):
                    par = fg % 2
                    t_w2 = p.dma("gpsimd",
                                 V(b.slab4, par * 4096,
                                   [[14336, 128], [1024, 4], [1, 1024]]),
                                 V(io["l1f_w2b"], fg * 4 * 262144 + wh * 1024,
                                   [[2048, 128], [262144, 4], [1, 1024]]),
                                 sem=("DW" if par == 0 else "DB"),
                                 waits=(prev_w.get(par, []) if wh or fg >= 2
                                        else x2free.get(par, [])
                                        + ev_prev))
                    for fl in range(4):
                        fi = fg * 4 + fl
                        for tl in range(4):
                            for half in range(2):
                                dv = (V(b.psA, tl * 1024 + half * 512,
                                        [[2048, 128], [1, 512]]) if tl < 2 else
                                      V(b.psB, half * 512,
                                        [[1024, 128], [1, 512]])
                                      if tl == 2 else
                                      V(b.psT, half * 1024,
                                        [[2048, 128], [1, 1024]]).bitcast(F32))
                                w2w = [t_w2]
                                if fi == 0:
                                    w2w += ev_prev
                                mlast[tl] = p.mm(
                                    dv,
                                    V(b.slab1, fi * 512 + tl * 128,
                                      [[32768, 128], [1, 128]]),
                                    V(b.slab4, par * 4096 + fl * 1024
                                      + half * 512,
                                      [[14336, 128], [1, 512]]),
                                    start=(fi == 0), stop=False, waits=w2w)
                    prev_w[par] = [mlast[3]]
                for tl in range(4):
                    tg = tq * 4 + tl
                    for ec in range(2):
                        for half in range(2):
                            dv = (V(b.psA, tl * 1024 + half * 512,
                                    [[2048, 128], [1, 512]]) if tl < 2 else
                                  V(b.psB, half * 512, [[1024, 128], [1, 512]])
                                  if tl == 2 else
                                  V(b.psT, half * 1024, [[2048, 128], [1, 1024]]
                                    ).bitcast(F32))
                            mlast[tl] = p.mm(
                                dv,
                                V(b.slab4, 8192 + ec * 2048 + tg * 128,
                                  [[14336, 128], [1, 128]]),
                                s6(ec * 2048 + wh * 1024 + half * 512,
                                   [[6144, 128], [1, 512]]),
                                start=False,
                                stop=(ec == 1), waits=[t_b2])
                evs2 = []
                for tl in range(4):
                    tg = tq * 4 + tl
                    rw = b.cols[:, rwoff + tg:rwoff + tg + 1]
                    src = psd[tl] if tl < 3 else psd3
                    t_sc = p.act(V(b.slab3, tl * 1024, [[4096, 128], [1, 1024]]),
                                 src, AF.Identity,
                                 waits=[mlast[tl]] + ([stage_dma[tl]]
                                                      if stage_dma[tl] else []),
                                 scale=rw)
                    evs2.append(t_sc)
                    stage_dma[tl] = p.dma(
                        "gpsimd",
                        V(io["acc"], tg * 128 * 2048 + wh * 1024,
                          [[2048, 128], [1, 1024]]),
                        V(b.slab3, tl * 1024, [[4096, 128], [1, 1024]]),
                        waits=[t_sc], sem="DA%d" % tl,
                        accum_op=(ALU.add if k > 0 else ALU.bypass))
                ev_prev = evs2
    p.barrier()
    xt = V(b.slab3, 0, [[4096, 128], [1, 2048]])
    mo = V(b.slab3, 2048, [[4096, 128], [1, 2048]])
    prev = []
    for i in range(NT):
        tx = p.dma("sync", xt, io["x_in"][i * 128:(i + 1) * 128, :],
                   waits=prev, sem="DX")
        tm = p.dma("sync", mo, io["acc"][i * 128:(i + 1) * 128, :],
                   waits=prev, sem="DR")
        ta = p.tt(xt, xt, mo, ALU.add, waits=[tx, tm])
        ts_ = p.dma("sync", io["x_out"][i * 128:(i + 1) * 128, :], xt,
                    waits=[ta], sem="DS")
        prev = [ta, ts_]
    p.barrier()


def _cat(*a):
    return np.ascontiguousarray(np.concatenate(a, axis=-1))


def prep_inputs(inputs):
    g = {}
    f32 = np.float32
    asf = lambda x: np.ascontiguousarray(np.asarray(x, dtype=f32))
    g["eye"] = np.eye(128, dtype=f32)
    mu = np.where(np.arange(128)[:, None] < np.arange(128)[None, :], NEG, 0.0)
    ml = np.where(np.arange(128)[:, None] > np.arange(128)[None, :], NEG, 0.0)
    g["masks"] = np.ascontiguousarray(
        np.stack([mu, ml]).astype(f32))
    Wq, Wk, Wv = (asf(inputs["L0_attn_Wq"]), asf(inputs["L0_attn_Wk"]),
                  asf(inputs["L0_attn_Wv"]))
    g["l0_wqkv"] = np.ascontiguousarray(np.stack(
        [_cat(Wq[:, h * 128:(h + 1) * 128], Wk[:, h * 128:(h + 1) * 128],
              Wv[:, h * 128:(h + 1) * 128]) for h in range(NH)]))
    bq, bk, bv = (asf(inputs["L0_attn_bq"]), asf(inputs["L0_attn_bk"]),
                  asf(inputs["L0_attn_bv"]))
    g["l0_bqkv"] = np.ascontiguousarray(np.stack(
        [_cat(bq[h * 128:(h + 1) * 128], bk[h * 128:(h + 1) * 128],
              bv[h * 128:(h + 1) * 128]) for h in range(NH)]))
    g["l0_wo"] = asf(inputs["L0_attn_Wo"])
    g["l0_bo"] = asf(inputs["L0_attn_bo"]).reshape(1, DM)
    g["l0_gb"] = np.ascontiguousarray(np.stack(
        [np.broadcast_to(asf(inputs["L0_attn_ln_g"]), (128, DM)),
         np.broadcast_to(asf(inputs["L0_attn_ln_b"]), (128, DM))]))
    g["l0f_gws"] = _cat(asf(inputs["L0_ffn_gate_w"]).T,
                        asf(inputs["L0_ffn_shared_gate_w"]))
    wg, wu = asf(inputs["L0_ffn_w_gate"]), asf(inputs["L0_ffn_w_up"])
    wd = asf(inputs["L0_ffn_w_down"])
    swg, swu = asf(inputs["L0_ffn_sw_gate"]), asf(inputs["L0_ffn_sw_up"])
    swd = asf(inputs["L0_ffn_sw_down"])
    wgu = np.empty((12, 11, DM, 256), np.float32)
    wdd = np.empty((12, 2, 1408, 1024), np.float32)
    for e in range(12):
        for ic in range(11):
            sl = slice(ic * 128, (ic + 1) * 128)
            if e < 8:
                wgu[e, ic, :, 0:128] = wg[e][:, sl]
                wgu[e, ic, :, 128:256] = wu[e][:, sl]
            else:
                j = e - 8
                wgu[e, ic, :, 0:128] = swg[:, j * 1408 + ic * 128:
                                           j * 1408 + (ic + 1) * 128]
                wgu[e, ic, :, 128:256] = swu[:, j * 1408 + ic * 128:
                                             j * 1408 + (ic + 1) * 128]
        for fh in range(2):
            sl = slice(fh * 1024, (fh + 1) * 1024)
            if e < 8:
                wdd[e, fh] = wd[e][:, sl]
            else:
                j = e - 8
                wdd[e, fh] = swd[j * 1408:(j + 1) * 1408, sl]
    g["l0f_wgu"] = wgu
    g["l0f_wd"] = wdd
    Wq1, Wk1, Wv1 = (asf(inputs["L1_attn_Wq"]), asf(inputs["L1_attn_Wk"]),
                     asf(inputs["L1_attn_Wv"]))
    g["l1_wqkv"] = np.ascontiguousarray(np.stack(
        [_cat(Wq1[:, h * 128:(h + 1) * 128], Wk1[:, h * 128:(h + 1) * 128],
              Wv1[:, h * 128:(h + 1) * 128]) for h in range(NH)]))
    g["l1_wo"] = asf(inputs["L1_attn_Wo"])
    g["l1_betas"] = np.ascontiguousarray(
        asf(inputs["L1_attn_betas"])[0, :, 0, :].T)
    W1, W2, W3 = (asf(inputs["L1_ffn_W1"]), asf(inputs["L1_ffn_W2"]),
                  asf(inputs["L1_ffn_W3"]))
    g["l1f_wgt"] = np.ascontiguousarray(asf(inputs["L1_ffn_Wg"]).T)
    g["l1f_w1b"] = np.ascontiguousarray(
        W1.T.reshape(2048, 56, 128).transpose(1, 0, 2))
    g["l1f_w3b"] = np.ascontiguousarray(
        W3.T.reshape(2048, 56, 128).transpose(1, 0, 2))
    g["l1f_w2b"] = np.ascontiguousarray(W2.T.reshape(56, 128, 2048))
    B1f = asf(inputs["L1_ffn_B1"]).transpose(0, 2, 1).reshape(256, 7168)
    B3f = asf(inputs["L1_ffn_B3"]).transpose(0, 2, 1).reshape(256, 7168)
    A2bt = asf(inputs["L1_ffn_A2"]).reshape(256, 7168).T.reshape(56, 128, 256)
    pack = np.empty((56, 128, 768), np.float32)
    b1b = B1f.reshape(256, 56, 128).transpose(1, 0, 2)
    b3b = B3f.reshape(256, 56, 128).transpose(1, 0, 2)
    for fc in range(56):
        pack[fc, :, 0:256] = b1b[fc].reshape(2, 128, 128).transpose(1, 0, 2
                                                                    ).reshape(128, 256)
        pack[fc, :, 256:512] = b3b[fc].reshape(2, 128, 128).transpose(1, 0, 2
                                                                      ).reshape(128, 256)
        pack[fc, :, 512:768] = A2bt[fc]
    g["l1f_pack"] = pack
    g["l1f_b2f"] = np.ascontiguousarray(
        asf(inputs["L1_ffn_B2"]).transpose(0, 2, 1).reshape(256, 2048))
    g["l1f_a1t"] = np.ascontiguousarray(
        asf(inputs["L1_ffn_A1"]).reshape(256, 2048).T)
    g["l1f_a3t"] = np.ascontiguousarray(
        asf(inputs["L1_ffn_A3"]).reshape(256, 2048).T)
    g["l1f_a2b"] = np.ascontiguousarray(
        asf(inputs["L1_ffn_A2"]).reshape(256, 7168).T.reshape(56, 128, 256))
    return g


def build_phase(phase):
    nc = bass.Bass("TRN2", target_bir_lowering=False)
    io = {}
    io["x_in"] = nc.dram_tensor("x_in", [S, DM], F32, kind="ExternalInput")
    io["x_out"] = nc.dram_tensor("x_out", [S, DM], F32, kind="ExternalOutput")
    io["eye"] = nc.dram_tensor("eye", [128, 128], F32, kind="ExternalInput")
    io["masks"] = nc.dram_tensor("masks", [2, 128, 128], F32,
                                 kind="ExternalInput")
    if phase == "l0attn":
        io["l0_wqkv"] = nc.dram_tensor("l0_wqkv", [NH, DM, 384], F32,
                                       kind="ExternalInput")
        io["l0_bqkv"] = nc.dram_tensor("l0_bqkv", [NH, 384], F32,
                                       kind="ExternalInput")
        io["l0_wo"] = nc.dram_tensor("l0_wo", [DM, DM], F32,
                                     kind="ExternalInput")
        io["l0_bo"] = nc.dram_tensor("l0_bo", [1, DM], F32,
                                     kind="ExternalInput")
        io["l0_gb"] = nc.dram_tensor("l0_gb", [2, 128, DM], F32,
                                     kind="ExternalInput")
    if phase in ("l1attn", "l1attn_dbg", "l1attn_dbg2"):
        io["l1_wqkv"] = nc.dram_tensor("l1_wqkv", [NH, DM, 384], F32,
                                       kind="ExternalInput")
        io["l1_wo"] = nc.dram_tensor("l1_wo", [DM, DM], F32,
                                     kind="ExternalInput")
        io["l1_betas"] = nc.dram_tensor("l1_betas", [128, NH], F32,
                                        kind="ExternalInput")
    if phase == "l0ffn":
        io["l0f_gws"] = nc.dram_tensor("l0f_gws", [DM, 9], F32,
                                       kind="ExternalInput")
        io["l0f_wgu"] = nc.dram_tensor("l0f_wgu", [12, 11, DM, 256], F32,
                                       kind="ExternalInput")
        io["l0f_wd"] = nc.dram_tensor("l0f_wd", [12, 2, 1408, 1024], F32,
                                      kind="ExternalInput")
        io["moe"] = nc.dram_tensor("moe_scratch", [S, DM], F32)
    if phase == "l1ffn":
        io["l1f_wgt"] = nc.dram_tensor("l1f_wgt", [DM, 8], F32,
                                       kind="ExternalInput")
        io["l1f_w1b"] = nc.dram_tensor("l1f_w1b", [56, DM, 128], F32,
                                       kind="ExternalInput")
        io["l1f_w3b"] = nc.dram_tensor("l1f_w3b", [56, DM, 128], F32,
                                       kind="ExternalInput")
        io["l1f_w2b"] = nc.dram_tensor("l1f_w2b", [56, 128, DM], F32,
                                       kind="ExternalInput")
        io["l1f_pack"] = nc.dram_tensor("l1f_pack", [56, 128, 768], F32,
                                        kind="ExternalInput")
        io["l1f_b2f"] = nc.dram_tensor("l1f_b2f", [256, DM], F32,
                                       kind="ExternalInput")
        io["l1f_a1t"] = nc.dram_tensor("l1f_a1t", [DM, 256], F32,
                                       kind="ExternalInput")
        io["l1f_a3t"] = nc.dram_tensor("l1f_a3t", [DM, 256], F32,
                                       kind="ExternalInput")
        io["base1"] = nc.dram_tensor("base1_scr", [128, 56 * 2048], BF16)
        io["base3"] = nc.dram_tensor("base3_scr", [128, 56 * 2048], BF16)
        io["acc"] = nc.dram_tensor("acc_scr", [S, DM], F32)
    with ExitStack() as ctx:
        p = P(nc)
        p.make_sems(ctx)
        b = Bufs(nc, ctx)
        if phase == "l0attn":
            ph_l0attn(p, b, io)
        if phase == "l0ffn":
            ph_l0ffn(p, b, io)
        if phase == "l1attn":
            ph_l1attn(p, b, io)
        if phase == "l1attn_dbg":
            ph_l1attn(p, b, io, dbg=True)
        if phase == "l1ffn":
            ph_l1ffn(p, b, io)
        p.finalize()
    global LAST_CNT
    LAST_CNT = dict(p.cnt)
    return nc


def build_full():
    nc = bass.Bass("TRN2", target_bir_lowering=False)
    io = {}
    io["x_in"] = nc.dram_tensor("x_in", [S, DM], F32, kind="ExternalInput")
    io["x_out"] = nc.dram_tensor("x_out", [S, DM], F32, kind="ExternalOutput")
    io["eye"] = nc.dram_tensor("eye", [128, 128], F32, kind="ExternalInput")
    io["masks"] = nc.dram_tensor("masks", [2, 128, 128], F32,
                                 kind="ExternalInput")
    for name, shape in (("l0_wqkv", [NH, DM, 384]), ("l0_bqkv", [NH, 384]),
                        ("l0_wo", [DM, DM]), ("l0_bo", [1, DM]),
                        ("l0_gb", [2, 128, DM]), ("l0f_gws", [DM, 9]),
                        ("l0f_wgu", [12, 11, DM, 256]),
                        ("l0f_wd", [12, 2, 1408, 1024]),
                        ("l1_wqkv", [NH, DM, 384]), ("l1_wo", [DM, DM]),
                        ("l1_betas", [128, NH]), ("l1f_wgt", [DM, 8]),
                        ("l1f_w1b", [56, DM, 128]), ("l1f_w3b", [56, DM, 128]),
                        ("l1f_w2b", [56, 128, DM]),
                        ("l1f_pack", [56, 128, 768]), ("l1f_b2f", [256, DM]),
                        ("l1f_a1t", [DM, 256]), ("l1f_a3t", [DM, 256])):
        io[name] = nc.dram_tensor(name, shape, F32, kind="ExternalInput")
    x1 = nc.dram_tensor("x1_scr", [S, DM], F32)
    x2 = nc.dram_tensor("x2_scr", [S, DM], F32)
    x3 = nc.dram_tensor("x3_scr", [S, DM], F32)
    io["moe"] = nc.dram_tensor("moe_scr", [S, DM], F32)
    io["base1"] = nc.dram_tensor("base1_scr", [128, 56 * 2048], BF16)
    io["base3"] = nc.dram_tensor("base3_scr", [128, 56 * 2048], BF16)
    io["acc"] = nc.dram_tensor("acc_scr", [S, DM], F32)
    x_final = io["x_out"]
    with ExitStack() as ctx:
        p = P(nc)
        p.make_sems(ctx)
        b = Bufs(nc, ctx)
        io["x_out"] = x1
        ph_l0attn(p, b, io)
        io["x_in"], io["x_out"] = x1, x2
        ph_l0ffn(p, b, io)
        io["x_in"], io["x_out"] = x2, x3
        ph_l1attn(p, b, io)
        io["x_in"], io["x_out"] = x3, x_final
        ph_l1ffn(p, b, io)
        p.finalize()
    global LAST_CNT
    LAST_CNT = dict(p.cnt)
    return nc


PHASE_KEYS = {
    "l0attn": ["eye", "masks", "l0_wqkv", "l0_bqkv", "l0_wo", "l0_bo",
               "l0_gb"],
    "l0ffn": ["eye", "masks", "l0f_gws", "l0f_wgu", "l0f_wd"],
    "l1attn": ["eye", "masks", "l1_wqkv", "l1_wo", "l1_betas"],
    "l1attn_dbg": ["eye", "masks", "l1_wqkv", "l1_wo", "l1_betas"],
    "l1attn_dbg2": ["eye", "masks", "l1_wqkv", "l1_wo", "l1_betas"],
    "l1ffn": ["eye", "masks", "l1f_wgt", "l1f_w1b", "l1f_w3b", "l1f_w2b",
              "l1f_pack", "l1f_b2f", "l1f_a1t", "l1f_a3t"],
}


ALL_KEYS = ["eye", "masks", "l0_wqkv", "l0_bqkv", "l0_wo", "l0_bo", "l0_gb",
            "l0f_gws", "l0f_wgu", "l0f_wd", "l1_wqkv", "l1_wo", "l1_betas",
            "l1f_wgt", "l1f_w1b", "l1f_w3b", "l1f_w2b", "l1f_pack",
            "l1f_b2f", "l1f_a1t", "l1f_a3t"]


def kernel(**inputs):
    g = prep_inputs(inputs)
    x = [np.ascontiguousarray(np.asarray(inputs["L0_attn_X"][bb],
                                         dtype=np.float32))
         for bb in range(8)]
    if FUSED:
        nc = build_full()
        in_maps = []
        for bb in range(8):
            m = {k: g[k] for k in ALL_KEYS}
            m["x_in"] = x[bb]
            in_maps.append(m)
        res = run_bass_kernel_spmd(nc, in_maps, core_ids=list(range(8)))
        return np.stack([np.asarray(r["x_out"])
                         for r in res.results]).astype(np.float32)
    for phase in ["l0attn", "l0ffn", "l1attn", "l1ffn"]:
        nc = build_phase(phase)
        in_maps = []
        for bb in range(8):
            m = {k: g[k] for k in PHASE_KEYS[phase]}
            m["x_in"] = x[bb]
            in_maps.append(m)
        res = run_bass_kernel_spmd(nc, in_maps, core_ids=list(range(8)))
        x = [np.asarray(r["x_out"]) for r in res.results]
    return np.stack(x).astype(np.float32)
